# version 07
# baseline (speedup 1.0000x reference)
import math
from contextlib import ExitStack

import numpy as np

import concourse.bass as bass
import concourse.mybir as mybir
from concourse.bass_utils import run_bass_kernel_spmd

F32 = mybir.dt.float32
AF = mybir.ActivationFunctionType
ALU = mybir.AluOpType
AX = mybir.AxisListType

ENGS = ("sync", "tensor", "vector", "scalar", "gpsimd")


class Cfg:
    def __init__(self, world=8, pc=2048):
        self.W = world
        self.PC = pc
        self.B = 2
        self.D = 1024
        self.R = self.B * pc
        self.NT = self.R // 128
        self.PT = self.PC // 128
        self.N5 = self.R // 512
        self.E = 8
        self.K = 2
        self.F = 2048
        self.QKVO = 1536
        N = self.B * self.W * pc
        self.CAP = int(np.ceil(N * self.K / self.E * 1.25))
        self.DI = 2048
        self.DS = 128
        self.MH = 16
        self.DC = 4
        self.CH = 256
        self.NC = pc // self.CH
        self.CONV = self.DI + 2 * self.DS
        self.DIP = 2 * self.DI + 2 * self.DS + self.MH
        self.EPS = 1e-5


class Bld:
    def __init__(self, nc):
        self.nc = nc
        self.q = {e: [] for e in ENGS}
        self.sems = {}
        self.cnt = {}
        self.nbar = 0
        self.dma_owner = {}
        self.drained = {}

    def _sem(self, name):
        if name not in self.sems:
            self.sems[name] = self.nc.alloc_semaphore(name)
            self.cnt[name] = 0
        return self.sems[name]

    def emit(self, eng, fn, waits=(), inc=None, amt=1):
        ws = []
        flat = []
        stack = list(waits)
        while stack:
            t = stack.pop()
            if isinstance(t, list):
                stack.extend(t)
            else:
                flat.append(t)
        best = {}
        for t in flat:
            if t is None:
                continue
            s, v = t
            if best.get(s, -1) < v:
                best[s] = v
        for s, v in best.items():
            assert self.cnt.get(s, -1) >= v, f"wait {s}>={v} not scheduled (have {self.cnt.get(s)})"
            ws.append((self._sem(s), v))
        tok = None
        ia = None
        if inc:
            sm = self._sem(inc)
            self.cnt[inc] += amt
            ia = (sm, amt)
            tok = (inc, self.cnt[inc])
        self.q[eng].append((fn, ws, ia))
        return tok

    def dma(self, eng, out, in_, waits=(), sem=None, **kw):
        if sem is None:
            sem = {'sync': 'sdD', 'gpsimd': 'gdD', 'scalar': 'adD', 'vector': 'vdD'}[eng]
        own = self.dma_owner.setdefault(sem, eng)
        assert own == eng, f"dma sem {sem} owned by {own}, used from {eng}"
        return self.emit(eng, (lambda E, o=out, i=in_, k=kw: E.dma_start(out=o, in_=i, **k)),
                         waits, sem, 16)

    COMPUTE_OWNER = {'pe': 'tensor', 'dv': 'vector', 'ac': 'scalar', 'gp': 'gpsimd', 'cc': 'gpsimd'}

    def bar(self):
        own = dict(self.dma_owner)
        for sname, eng in self.COMPUTE_OWNER.items():
            if sname in self.sems:
                own[sname] = eng
        for sname, eng in own.items():
            v = self.cnt.get(sname, 0)
            if v > self.drained.get(sname, -1):
                self.q[eng].append(((lambda E, sm=self.sems[sname], tv=v: E.wait_ge(sm, tv)), [], None))
                self.drained[sname] = v
        self.nbar += 1
        self._sem('ph')
        self.cnt['ph'] += len(ENGS)
        t = len(ENGS) * self.nbar
        assert t == self.cnt['ph']
        for e in ENGS:
            self.q[e].append(((lambda E: E.sem_inc(self.sems['ph'], 1)), [], None))
            self.q[e].append(((lambda E, tv=t: E.wait_ge(self.sems['ph'], tv)), [], None))

    def build(self):
        with self.nc.Block() as block:
            for e in ENGS:
                getattr(block, e)(self._runner(e))

    def _runner(self, e):
        ops = self.q[e]

        def run(E):
            for fn, ws, ia in ops:
                for (sm, v) in ws:
                    E.wait_ge(sm, v)
                ins = fn(E)
                if ia is not None:
                    ins.then_inc(*ia)
        return run


class Pool:
    def __init__(self, tiles):
        self.tiles = tiles
        self.free_tok = [[] for _ in tiles]
        self.i = 0

    def get(self):
        i = self.i
        self.i = (self.i + 1) % len(self.tiles)
        return i, self.tiles[i], self.free_tok[i]

    def done(self, i, *toks):
        out = []
        for t in toks:
            if isinstance(t, list):
                out.extend(x for x in t if x is not None)
            elif t is not None:
                out.append(t)
        self.free_tok[i] = out


def build_program(cfg: Cfg, stub_mamba=False, dbg=False):
    nc = bass.Bass("TRN2", target_bir_lowering=False)
    c = cfg
    R, D = c.R, c.D
    b = Bld(nc)
    Hd = {}

    U = nc.dram_tensor("u", [R, D], F32, kind="ExternalInput").ap()
    OUT = nc.dram_tensor("out", [R, D], F32, kind="ExternalOutput").ap()
    WQKVT = nc.dram_tensor("wqkvT", [D, c.QKVO], F32, kind="ExternalInput").ap()
    WPROJT = nc.dram_tensor("wprojT", [D, D], F32, kind="ExternalInput").ap()
    WROUTER = nc.dram_tensor("wrouter", [D, c.E], F32, kind="ExternalInput").ap()
    MOEW1 = nc.dram_tensor("moew1", [c.E, D, c.F], F32, kind="ExternalInput").ap()
    MOEW2 = nc.dram_tensor("moew2", [c.E, c.F, D], F32, kind="ExternalInput").ap()
    LN1W = nc.dram_tensor("ln1w", [D], F32, kind="ExternalInput").ap()
    LN1B = nc.dram_tensor("ln1b", [D], F32, kind="ExternalInput").ap()
    LN2W = nc.dram_tensor("ln2w", [D], F32, kind="ExternalInput").ap()
    LN2B = nc.dram_tensor("ln2b", [D], F32, kind="ExternalInput").ap()
    WINT = nc.dram_tensor("winT", [D, c.DIP], F32, kind="ExternalInput").ap()
    WOUTT = nc.dram_tensor("woutT", [c.DI, D], F32, kind="ExternalInput").ap()
    CONVW = nc.dram_tensor("convw", [c.CONV, c.DC], F32, kind="ExternalInput").ap()
    CONVB = nc.dram_tensor("convb", [c.CONV], F32, kind="ExternalInput").ap()
    DTBIAS = nc.dram_tensor("dtbias", [c.MH], F32, kind="ExternalInput").ap()
    ALOG = nc.dram_tensor("alog", [c.MH], F32, kind="ExternalInput").ap()
    DVEC = nc.dram_tensor("dvec", [c.MH], F32, kind="ExternalInput").ap()
    MNW = nc.dram_tensor("mnw", [c.DI], F32, kind="ExternalInput").ap()
    CMASK = nc.dram_tensor("cmask", [c.W], F32, kind="ExternalInput").ap()
    HMASK = nc.dram_tensor("hmask", [1], F32, kind="ExternalInput").ap()

    dkind = dict(kind="ExternalOutput") if dbg else {}
    RES1 = nc.dram_tensor("res1", [R, D], F32, **dkind).ap()
    RES2 = nc.dram_tensor("res2", [R, D], F32, **dkind).ap()
    HB = nc.dram_tensor("hbuf", [R, D], F32, **dkind).ap()
    H1B = nc.dram_tensor("h1buf", [R, D], F32, **dkind).ap()
    QKVB = nc.dram_tensor("qkvbuf", [R, c.QKVO], F32, **dkind).ap()
    LN2T = nc.dram_tensor("ln2t", [D, R], F32, **dkind).ap()
    MOEB = nc.dram_tensor("moeb", [R, D], F32, **dkind).ap()
    Hd.update(dict(nc=nc, RES1=RES1, RES2=RES2, HB=HB, H1B=H1B, QKVB=QKVB, LN2T=LN2T, MOEB=MOEB))

    sbp = nc.alloc_sbuf_tensor
    ident = sbp("ident", [128, 128], F32)
    cmask_b = sbp("cmask_b", [128, c.W], F32)
    cmask_c = sbp("cmask_c", [c.W, 1], F32)
    hmask_b = sbp("hmask_b", [128, 1], F32)
    iot_c = sbp("iot_c", [128, 128], F32)
    iot_r = sbp("iot_r", [128, 128], F32)
    tril_s = sbp("tril_s", [128, 128], F32)
    ones1 = sbp("ones1", [1, 128], F32)
    eiota = sbp("eiota", [128, c.E], F32)
    em99 = sbp("em99", [128, c.E], F32)
    GMAT = sbp("gmat", [128, c.E * c.NT], F32)
    Hd['GMAT'] = GMAT

    PS = [nc.alloc_psum_tensor(f"ps{i}", [128, 512], F32) for i in range(8)]
    ppA = Pool(PS[0:3])
    ppT = Pool(PS[3:5])
    ppB = Pool(PS[5:8])

    def bcast(dst, src_ap, n, waits=(), sem=None):
        return b.dma('sync', dst[:, 0:n],
                     src_ap.rearrange("(o d) -> o d", o=1).broadcast_to([128, n]), waits, sem=sem)

    CONSTVALS = [1e-5, 1.0 / 1024.0, 1.0 / 2048.0, 0.125, -0.125, 1.0, -1.0, -1e9, 0.5, 2.0]
    cdb = sbp("cdb", [128, len(CONSTVALS)], F32)
    for i_, v_ in enumerate(CONSTVALS):
        nc.const_aps.register(F32, v_, cdb[:, i_:i_ + 1])

    def consts():
        t1 = b.emit('gpsimd', lambda E: E.iota(iot_c[:, :], [[1, 128]], channel_multiplier=0,
                                               allow_small_or_imprecise_dtypes=True), (), 'gp')
        t2 = b.emit('gpsimd', lambda E: E.iota(iot_r[:, :], [[0, 128]], channel_multiplier=1,
                                               allow_small_or_imprecise_dtypes=True), (), 'gp')
        t3 = b.emit('gpsimd', lambda E: E.iota(eiota[:, :], [[1, c.E]], channel_multiplier=0,
                                               allow_small_or_imprecise_dtypes=True), (), 'gp')
        b.emit('vector', lambda E: E.tensor_tensor(ident[:, :], iot_c[:, :], iot_r[:, :], ALU.is_equal),
               [t1, t2], 'dv')
        b.emit('vector', lambda E: E.tensor_tensor(tril_s[:, :], iot_r[:, :], iot_c[:, :], ALU.is_lt),
               [t1, t2], 'dv')
        b.emit('vector', lambda E: E.tensor_scalar(em99[:, :], eiota[:, :], 99.0, None, ALU.subtract),
               [t3], 'dv')
        b.emit('vector', lambda E: E.memset(ones1[:, :], 1.0), (), 'dv')
        for i_, v_ in enumerate(CONSTVALS):
            b.emit('vector', lambda E, i_=i_, v_=v_: E.memset(cdb[:, i_:i_ + 1], v_), (), 'dv')
        b.emit('vector', lambda E: E.memset(GMAT[:, :], 0.0), (), 'dv')

        bcast(cmask_b, CMASK, c.W)
        bcast(hmask_b, HMASK, 1)
        b.dma('sync', cmask_c[:, :], CMASK.rearrange("(w o) -> w o", o=1))
        b.bar()

    def ln_na(xa_ap, hn_ap, mn, ssq, sqt, rstd, waits, prev=None):
        t = b.emit('vector', lambda E: E.reduce_sum(mn, xa_ap, AX.X), list(waits) + [prev], 'dv')
        t = b.emit('scalar', lambda E: E.mul(mn, mn, 1.0 / D), [t], 'ac')
        t = b.emit('vector', lambda E: E.tensor_scalar(hn_ap, xa_ap, mn, None, ALU.subtract), [t], 'dv')
        t2 = b.emit('scalar', lambda E: E.activation(sqt, hn_ap, AF.Square, accum_out=ssq), [t], 'ac')
        t2 = b.emit('scalar', lambda E: E.activation(rstd, ssq, AF.Sqrt, bias=c.EPS, scale=1.0 / D), [t2], 'ac')
        t2 = b.emit('vector', lambda E: E.reciprocal(rstd, rstd), [t2], 'dv')
        t = b.emit('vector', lambda E: E.tensor_scalar(hn_ap, hn_ap, rstd, None, ALU.mult), [t2], 'dv')
        return t

    def transpose128(src_ap, dst_ap, waits):
        i, pt, ft = ppT.get()
        tk = b.emit('tensor', lambda E, s=src_ap, p=pt: E.transpose(p[:, 0:128], s, ident[:, :]),
                    list(waits) + [ft], 'pe')
        tk = b.emit('vector', lambda E, d=dst_ap, p=pt: E.tensor_copy(d, p[:, 0:128]), [tk], 'dv')
        ppT.done(i, tk)
        return tk

    def ffn_call(src, dst, call):
        es_c = ExitStack()
        sbc = lambda n, shp: es_c.enter_context(nc.sbuf_tensor(f"f{call}ln_{n}", shp, F32))
        ln1w_b = sbc("ln1w", [128, D]); ln1b_b = sbc("ln1b", [128, D])
        ln2w_b = sbc("ln2w", [128, D]); ln2b_b = sbc("ln2b", [128, D])
        lnt = [bcast(ln1w_b, LN1W, D, sem='lnc'), bcast(ln1b_b, LN1B, D, sem='lnc'),
               bcast(ln2w_b, LN2W, D, sem='lnc'), bcast(ln2b_b, LN2B, D, sem='lnc')]
        lntok = lnt[-1]
        es = ExitStack()
        sbx = lambda n, shp: es.enter_context(nc.sbuf_tensor(f"f{call}a_{n}", shp, F32))
        wq = sbx("wq", [128, 8 * c.QKVO])
        xa_p = Pool([sbx(f"xa{i}", [128, D]) for i in range(2)])
        hn_p = Pool([sbx(f"hn{i}", [128, D]) for i in range(2)])
        l1_p = Pool([sbx(f"l1_{i}", [128, D]) for i in range(2)])
        l1T_p = Pool([sbx(f"l1T{i}", [128, D]) for i in range(2)])
        qt_p = Pool([sbx(f"qt{i}", [128, c.QKVO]) for i in range(2)])
        mn = sbx("mn", [128, 1]); ssq = sbx("ssq", [128, 1]); rstd = sbx("rstd", [128, 1])
        sqt = sbx("sqt", [128, D])
        wtoks = [b.dma('sync', wq[:, d * c.QKVO:(d + 1) * c.QKVO], WQKVT[d * 128:(d + 1) * 128, :],
                       sem='wq') for d in range(8)]
        wtok = wtoks[-1]
        ln_prev = None
        for t in range(c.NT):
            xi, xa, xf = xa_p.get()
            tl = b.dma('sync', xa[:, :], src[t * 128:(t + 1) * 128, :], [xf], sem=f'f1x{xi}')
            hi, hn, hf = hn_p.get()
            tn = ln_na(xa[:, :], hn[:, :], mn[:, 0:1], ssq[:, 0:1], sqt[:, :], rstd[:, 0:1], [tl, hf],
                       prev=ln_prev)
            ln_prev = tn
            xa_p.done(xi, tn)
            th = b.dma('gpsimd', HB[t * 128:(t + 1) * 128, :], hn[:, :], [tn], sem=f'hbs{hi}')
            li1, l1, lf1 = l1_p.get()
            t2 = b.emit('vector', lambda E, hn=hn, l1=l1: E.tensor_tensor(l1[:, :], hn[:, :], ln1w_b[:, :], ALU.mult),
                        [tn] + lf1 + ([lntok] if t == 0 else []), 'dv')
            t2 = b.emit('vector', lambda E, l1=l1: E.tensor_tensor(l1[:, :], l1[:, :], ln1b_b[:, :], ALU.add),
                        [t2], 'dv')
            hn_p.done(hi, th)
            li, l1T, lf = l1T_p.get()
            ttk = [transpose128(l1[:, d * 128:(d + 1) * 128], l1T[:, d * 128:(d + 1) * 128], [t2, lf])
                   for d in range(8)]
            l1_p.done(li1, ttk[-1])
            qi, qt, qf = qt_p.get()
            evs = []
            for o in range(c.QKVO // 512):
                pi, ps, pf = ppA.get()
                mk = None
                for d in range(8):
                    w8 = [wtok] if t == 0 and o == 0 and d == 0 else []
                    mk = b.emit('tensor', lambda E, ps=ps, l1T=l1T, d=d, o=o: E.matmul(
                        ps[:, :], l1T[:, d * 128:(d + 1) * 128],
                        wq[:, d * c.QKVO + o * 512:d * c.QKVO + o * 512 + 512],
                        start=(d == 0), stop=(d == 7)),
                        ([ttk[d], pf] if d == 0 else [ttk[d]]) + w8, 'pe')
                ev = b.emit('scalar', lambda E, qt=qt, ps=ps, o=o: E.copy(qt[:, o * 512:(o + 1) * 512], ps[:, :]),
                            ([mk, qf] if o == 0 else [mk]), 'ac')
                evs.append(ev)
                ppA.done(pi, ev)
            l1T_p.done(li, evs[-1])
            tq = b.dma('gpsimd', QKVB[t * 128:(t + 1) * 128, :], qt[:, :], evs, sem=f'qts{qi}')
            qt_p.done(qi, tq)
        es.close()
        b.bar()

        es = ExitStack()
        sbx = lambda n, shp: es.enter_context(nc.sbuf_tensor(f"f{call}b_{n}", shp, F32))
        wp = sbx("wp", [128, 8 * D])
        q0_p = Pool([sbx(f"q0_{i}", [128, c.QKVO]) for i in range(2)])
        q1_p = Pool([sbx(f"q1_{i}", [128, c.QKVO]) for i in range(2)])
        ao_p = Pool([sbx(f"ao{i}", [128, 2 * D]) for i in range(2)])
        aoT_p = Pool([sbx(f"aoT{i}", [128, 2 * D]) for i in range(2)])
        sc = sbx("sc", [128, 80])
        prod = sbx("prod", [128, 256])
        hh_p = Pool([sbx(f"hh{i}", [128, D]) for i in range(2)])
        h1_p = Pool([sbx(f"h1_{i}", [128, D]) for i in range(2)])
        wtoks = [b.dma('sync', wp[:, d * D:(d + 1) * D], WPROJT[d * 128:(d + 1) * 128, :], sem='wp')
                 for d in range(8)]
        wtok = wtoks[-1]
        sc_free = []
        aot_last = []
        for j in range(c.PT):
            rows = (j * 128, c.PC + j * 128)
            i0, q0, f0 = q0_p.get()
            i1, q1, f1 = q1_p.get()
            tl0 = b.dma('sync', q0[:, :], QKVB[rows[0]:rows[0] + 128, :], [f0], sem=f'f2a{i0}')
            tl1 = b.dma('sync', q1[:, :], QKVB[rows[1]:rows[1] + 128, :], [f1], sem=f'f2b{i1}')
            sct = []
            prod_free = sc_free[-1:]
            for g in range(4):
                qs, ks = g * 384, g * 384 + 256
                for w_i, kbuf in ((0, q0), (1, q1)):
                    tm = b.emit('vector', lambda E, qs=qs, ks=ks, kbuf=kbuf, q1=q1: E.tensor_tensor(
                        prod[:, :].rearrange("p (a f) -> p a f", a=4),
                        q1[:, qs:qs + 256].rearrange("p (a f) -> p a f", a=4),
                        kbuf[:, ks:ks + 64].rearrange("p (o f) -> p o f", o=1).broadcast_to([128, 4, 64]),
                        ALU.mult), [tl0, tl1] + sc_free + prod_free, 'dv')
                    tm = b.emit('vector', lambda E, g=g, w_i=w_i: E.reduce_sum(
                        sc[:, w_i * 16 + g * 4:w_i * 16 + g * 4 + 4],
                        prod[:, :].rearrange("p (a f) -> p a f", a=4), AX.X), [tm], 'dv')
                    prod_free = [tm]
                    sct.append(tm)
            td = b.emit('vector', lambda E: E.tensor_tensor(sc[:, 32:48], sc[:, 0:16], sc[:, 16:32], ALU.subtract),
                        sct, 'dv')
            ta0 = b.emit('scalar', lambda E: E.activation(sc[:, 48:64], sc[:, 32:48], AF.Sigmoid, scale=0.125),
                         [td], 'ac')
            ta1 = b.emit('scalar', lambda E: E.activation(sc[:, 64:80], sc[:, 32:48], AF.Sigmoid, scale=-0.125),
                         [td], 'ac')
            sc_free = [ta0, ta1, td] + aot_last
            ai, ao, af = ao_p.get()
            aot = []
            for g in range(4):
                vs = g * 384 + 320
                tm = b.emit('vector', lambda E, g=g, vs=vs, ao=ao, q0=q0: E.tensor_copy(
                    ao[:, g * 256:(g + 1) * 256].rearrange("p (a f) -> p a f", a=4),
                    q0[:, vs:vs + 64].rearrange("p (o f) -> p o f", o=1).broadcast_to([128, 4, 64])),
                    [tl0, af], 'dv')
                aot.append(tm)
                for hj in range(4):
                    h_ = g * 4 + hj
                    tm1 = b.emit('vector', lambda E, vs=vs, h_=h_, ao=ao, q1=q1: E.tensor_scalar(
                        ao[:, D + h_ * 64:D + h_ * 64 + 64], q1[:, vs:vs + 64],
                        sc[:, 64 + h_:64 + h_ + 1], None, ALU.mult), [ta1, tl1, af], 'dv')
                    tm1 = b.emit('vector', lambda E, vs=vs, h_=h_, ao=ao, q0=q0: E.scalar_tensor_tensor(
                        ao[:, D + h_ * 64:D + h_ * 64 + 64], q0[:, vs:vs + 64],
                        sc[:, 48 + h_:48 + h_ + 1], ao[:, D + h_ * 64:D + h_ * 64 + 64],
                        ALU.mult, ALU.add), [ta0, tm1], 'dv')
                    aot.append(tm1)
            q0_p.done(i0, aot[-1])
            q1_p.done(i1, aot[-1])
            aot_last = [aot[-1]]
            ti, aoT, tf = aoT_p.get()
            ttk = [transpose128(ao[:, d * 128:(d + 1) * 128], aoT[:, d * 128:(d + 1) * 128],
                                [aot[-1], tf]) for d in range(16)]
            ao_p.done(ai, ttk[-1])
            evl = None
            for bb in range(2):
                hi2, hh, hf2 = hh_p.get()
                tlh = b.dma('sync', hh[:, :], HB[rows[bb]:rows[bb] + 128, :], [hf2], sem=f'f2h{hi2}')
                oi, h1t, of = h1_p.get()
                evs = []
                for o in range(2):
                    pi, ps, pf = ppA.get()
                    mk = None
                    for d in range(8):
                        w8 = [wtok] if j == 0 and bb == 0 and o == 0 and d == 0 else []
                        mk = b.emit('tensor', lambda E, ps=ps, aoT=aoT, bb=bb, d=d, o=o: E.matmul(
                            ps[:, :], aoT[:, bb * D + d * 128:bb * D + d * 128 + 128],
                            wp[:, d * D + o * 512:d * D + o * 512 + 512],
                            start=(d == 0), stop=(d == 7)),
                            ([ttk[bb * 8 + d], pf] if d == 0 else [ttk[bb * 8 + d]]) + w8, 'pe')
                    ev = b.emit('vector', lambda E, h1t=h1t, ps=ps, hh=hh, o=o: E.tensor_tensor(
                        h1t[:, o * 512:(o + 1) * 512], ps[:, :], hh[:, o * 512:(o + 1) * 512], ALU.add),
                        [mk, tlh] + ([of] if o == 0 else []), 'dv')
                    evs.append(ev)
                    ppA.done(pi, ev)
                hh_p.done(hi2, evs[-1])
                evl = b.dma('gpsimd', H1B[rows[bb]:rows[bb] + 128, :], h1t[:, :], evs, sem=f'h1s{oi}')
                h1_p.done(oi, evl)
            aoT_p.done(ti, evl)
        es.close()
        b.bar()

        es = ExitStack()
        sbx = lambda n, shp: es.enter_context(nc.sbuf_tensor(f"f{call}c_{n}", shp, F32))
        wr = sbx("wr", [128, 8 * c.E])
        xa_p = Pool([sbx(f"xa{i}", [128, D]) for i in range(2)])
        l2_p = Pool([sbx(f"l2_{i}", [128, D]) for i in range(2)])
        l2T_p = Pool([sbx(f"l2T{i}", [128, D]) for i in range(2)])
        mn = sbx("mn", [128, 1]); ssq = sbx("ssq", [128, 1]); rstd = sbx("rstd", [128, 1])
        sqt = sbx("sqt", [128, D])
        LG = sbx("lg", [128, c.E * c.NT])
        PR = sbx("pr", [128, c.E * c.NT])
        SM1 = sbx("sm1", [128, c.NT]); SM2 = sbx("sm2", [128, c.NT])
        I1 = sbx("i1", [128, c.NT]); I2 = sbx("i2", [128, c.NT])
        T1 = sbx("t1", [128, c.NT]); T2 = sbx("t2", [128, c.NT])
        C1 = sbx("c1", [128, c.E * c.NT]); C2 = sbx("c2", [128, c.E * c.NT])
        CM = sbx("cm", [128, c.E * c.NT])
        SCR = sbx("scr", [128, c.E * c.NT])
        CS = sbx("cs", [128, 16])
        VROW = sbx("vrow", [1, c.E * c.NT])
        TOTR = sbx("totr", [1, c.E * c.NT])
        PBR = sbx("pbr", [1, 16])
        AGC_sb = sbx("agc_sb", [max(c.W, 2), 16])
        RK1 = sbx("rk1", [128, c.NT]); RK2 = sbx("rk2", [128, c.NT])
        Hd.update(dict(LG=LG, PR=PR, I1=I1, I2=I2, T1=T1, T2=T2, RK1=RK1, RK2=RK2, C1=C1, C2=C2))
        wrt = [b.dma('sync', wr[:, d * c.E:(d + 1) * c.E], WROUTER[d * 128:(d + 1) * 128, :], sem='wr')
               for d in range(8)]
        wrtok = wrt[-1]
        rtoks = []
        ln_prev = None
        for t in range(c.NT):
            xi, xa, xf = xa_p.get()
            tl = b.dma('sync', xa[:, :], H1B[t * 128:(t + 1) * 128, :], [xf], sem=f'f3x{xi}')
            li, l2, lf = l2_p.get()
            tn = ln_na(xa[:, :], l2[:, :], mn[:, 0:1], ssq[:, 0:1], sqt[:, :], rstd[:, 0:1], [tl, lf],
                       prev=ln_prev)
            ln_prev = tn
            xa_p.done(xi, tn)
            t2 = b.emit('vector', lambda E, l2=l2: E.tensor_tensor(l2[:, :], l2[:, :], ln2w_b[:, :], ALU.mult),
                        [tn] + ([lntok] if t == 0 else []), 'dv')
            t2 = b.emit('vector', lambda E, l2=l2: E.tensor_tensor(l2[:, :], l2[:, :], ln2b_b[:, :], ALU.add),
                        [t2], 'dv')
            ti, l2T, tf = l2T_p.get()
            ttk = [transpose128(l2[:, d * 128:(d + 1) * 128], l2T[:, d * 128:(d + 1) * 128], [t2, tf])
                   for d in range(8)]
            l2_p.done(li, ttk[-1])
            ts = b.dma('gpsimd', LN2T[0:D, t * 128:(t + 1) * 128].rearrange("(d p) s -> p d s", p=128),
                       l2T[:, :].rearrange("p (d s) -> p d s", d=8), ttk, sem=f'l2s{ti}')
            pi, ps, pf = ppB.get()
            mk = None
            for d in range(8):
                w8 = [wrtok] if t == 0 and d == 0 else []
                mk = b.emit('tensor', lambda E, ps=ps, l2T=l2T, d=d: E.matmul(
                    ps[:, 0:c.E], l2T[:, d * 128:(d + 1) * 128], wr[:, d * c.E:(d + 1) * c.E],
                    start=(d == 0), stop=(d == 7)),
                    ([ttk[d], pf] if d == 0 else [ttk[d]]) + w8, 'pe')
            l2T_p.done(ti, ts, mk)
            ev = b.emit('vector', lambda E, ps=ps, t=t: E.tensor_copy(
                LG[:, t:c.E * c.NT:c.NT].rearrange("p (e o) -> p e o", o=1),
                ps[:, 0:c.E].rearrange("p (e o) -> p e o", o=1)), [mk, ts], 'dv')
            rtoks.append(ev)
            ppB.done(pi, ev)
        v3 = lambda ap: ap.rearrange("p (e t) -> p t e", e=c.E)
        tk = b.emit('vector', lambda E: E.reduce_max(SM1[:, :], v3(LG[:, :]), AX.X, negate=True), rtoks, 'dv')
        tk = b.emit('vector', lambda E: E.tensor_tensor(
            SCR[:, :].rearrange("p (e t) -> p t e", e=c.E), v3(LG[:, :]),
            SM1[:, :].rearrange("p (t o) -> p t o", o=1).broadcast_to([128, c.NT, c.E]), ALU.add), [tk], 'dv')
        tk = b.emit('scalar', lambda E: E.activation(SCR[:, :], SCR[:, :], AF.Exp), [tk], 'ac')
        tk = b.emit('vector', lambda E: E.reduce_sum(SM2[:, :], v3(SCR[:, :]), AX.X), [tk], 'dv')
        tk = b.emit('vector', lambda E: E.reciprocal(SM2[:, :], SM2[:, :]), [tk], 'dv')
        tk = b.emit('vector', lambda E: E.tensor_tensor(
            v3(PR[:, :]), v3(SCR[:, :]),
            SM2[:, :].rearrange("p (t o) -> p t o", o=1).broadcast_to([128, c.NT, c.E]), ALU.mult), [tk], 'dv')
        tk = b.emit('vector', lambda E: E.reduce_max(T1[:, :], v3(PR[:, :]), AX.X), [tk], 'dv')
        tk1 = b.emit('vector', lambda E: E.tensor_tensor(
            v3(C1[:, :]), v3(PR[:, :]),
            T1[:, :].rearrange("p (t o) -> p t o", o=1).broadcast_to([128, c.NT, c.E]), ALU.is_ge), [tk], 'dv')
        tk1 = b.emit('vector', lambda E: E.tensor_tensor(
            v3(SCR[:, :]), v3(C1[:, :]),
            em99[:, :].rearrange("p (o e) -> p o e", o=1).broadcast_to([128, c.NT, c.E]), ALU.mult), [tk1], 'dv')
        tk1 = b.emit('vector', lambda E: E.tensor_scalar(SCR[:, :], SCR[:, :], 99.0, None, ALU.add), [tk1], 'dv')
        tk1 = b.emit('vector', lambda E: E.tensor_reduce(I1[:, :], v3(SCR[:, :]), AX.X, ALU.min), [tk1], 'dv')
        tk1 = b.emit('vector', lambda E: E.tensor_tensor(
            v3(C1[:, :]),
            eiota[:, :].rearrange("p (o e) -> p o e", o=1).broadcast_to([128, c.NT, c.E]),
            I1[:, :].rearrange("p (t o) -> p t o", o=1).broadcast_to([128, c.NT, c.E]), ALU.is_equal), [tk1], 'dv')
        tk1 = b.emit('vector', lambda E: E.scalar_tensor_tensor(
            SCR[:, :], C1[:, :], -2.0, PR[:, :], ALU.mult, ALU.add), [tk1], 'dv')
        tk2 = b.emit('vector', lambda E: E.reduce_max(T2[:, :], v3(SCR[:, :]), AX.X), [tk1], 'dv')
        tk2 = b.emit('vector', lambda E: E.tensor_tensor(
            v3(C2[:, :]), v3(SCR[:, :]),
            T2[:, :].rearrange("p (t o) -> p t o", o=1).broadcast_to([128, c.NT, c.E]), ALU.is_ge), [tk2], 'dv')
        tk2 = b.emit('vector', lambda E: E.tensor_tensor(
            v3(SCR[:, :]), v3(C2[:, :]),
            em99[:, :].rearrange("p (o e) -> p o e", o=1).broadcast_to([128, c.NT, c.E]), ALU.mult), [tk2], 'dv')
        tk2 = b.emit('vector', lambda E: E.tensor_scalar(SCR[:, :], SCR[:, :], 99.0, None, ALU.add), [tk2], 'dv')
        tk2 = b.emit('vector', lambda E: E.tensor_reduce(I2[:, :], v3(SCR[:, :]), AX.X, ALU.min), [tk2], 'dv')
        tk2 = b.emit('vector', lambda E: E.tensor_tensor(
            v3(C2[:, :]),
            eiota[:, :].rearrange("p (o e) -> p o e", o=1).broadcast_to([128, c.NT, c.E]),
            I2[:, :].rearrange("p (t o) -> p t o", o=1).broadcast_to([128, c.NT, c.E]), ALU.is_equal), [tk2], 'dv')
        tcm = b.emit('vector', lambda E: E.tensor_tensor(CM[:, :], C1[:, :], C2[:, :], ALU.add), [tk1, tk2], 'dv')
        half = c.NT // 2
        cst = []
        for e_ in range(c.E):
            for blk in range(2):
                cst.append(b.emit('vector', lambda E, e_=e_, blk=blk: E.reduce_sum(
                    CS[:, e_ * 2 + blk:e_ * 2 + blk + 1],
                    CM[:, e_ * c.NT + blk * half:e_ * c.NT + (blk + 1) * half], AX.X), [tcm], 'dv'))
        pi_, psc, pf_ = ppB.get()
        mk = b.emit('tensor', lambda E, ps=psc: E.matmul(ps[0:1, 0:16], onesc_l(), CS[:, :],
                                                         start=True, stop=True), cst + [pf_], 'pe')
        tkc = b.emit('vector', lambda E, ps=psc: E.tensor_copy(AGC_sb[0:1, :], ps[0:1, 0:16]), [mk], 'dv')
        ppB.done(pi_, tkc)
        if c.W > 1:
            tg = b.dma('gpsimd', AGC_IN[call][:, :], AGC_sb[0:1, :], [tkc], sem='agi')
            tg = b.emit('gpsimd', lambda E: E.collective_compute(
                "AllGather", ALU.bypass, replica_groups=[list(range(c.W))],
                ins=[AGC_IN[call].opt()], outs=[AGC_OUT[call].opt()]), [tg], 'cc')
            tg = b.dma('sync', AGC_sb[0:c.W, :], AGC_OUT[call][:, :], [tg], sem='ago')
            pi2, ps2, pf2 = ppB.get()
            mk = b.emit('tensor', lambda E, ps=ps2: E.matmul(ps[0:1, 0:16], cmask_c[:, :], AGC_sb[0:c.W, :],
                                                             start=True, stop=False), [tg, pf2], 'pe')
            mk = b.emit('tensor', lambda E, ps=ps2: E.matmul(ps[0:1, 16:32], onesc_lw(), AGC_sb[0:c.W, :],
                                                             start=False, stop=True), [tg], 'pe')
            tpb = b.emit('vector', lambda E, ps=ps2: E.tensor_copy(PBR[0:1, 0:16], ps[0:1, 0:16]), [mk], 'dv')
            tpb = b.emit('vector', lambda E, ps=ps2: E.tensor_tensor(
                PBR[0:1, 1:16:2].rearrange("p (e o) -> p e o", o=1),
                PBR[0:1, 1:16:2].rearrange("p (e o) -> p e o", o=1),
                ps[0:1, 16:32:2].rearrange("p (e o) -> p e o", o=1), ALU.add), [tpb], 'dv')
            ppB.done(pi2, tpb)
        else:
            tpb = b.emit('vector', lambda E: E.memset(PBR[0:1, :], 0.0), [tkc], 'dv')
            tpb = b.emit('vector', lambda E: E.tensor_tensor(
                PBR[0:1, 1:16:2].rearrange("p (e o) -> p e o", o=1),
                PBR[0:1, 1:16:2].rearrange("p (e o) -> p e o", o=1),
                AGC_sb[0:1, 0:16:2].rearrange("p (e o) -> p e o", o=1), ALU.add), [tpb], 'dv')
        pi3, ps3, pf3 = ppB.get()
        mk = b.emit('tensor', lambda E, ps=ps3: E.matmul(ps[0:1, 0:c.E * c.NT], onesc_l(), CM[:, :],
                                                         start=True, stop=True), [tcm, pf3], 'pe')
        tto = b.emit('vector', lambda E, ps=ps3: E.tensor_copy(TOTR[0:1, :], ps[0:1, 0:c.E * c.NT]), [mk], 'dv')
        ppB.done(pi3, tto)
        vt = []
        for e_ in range(c.E):
            for blk in range(2):
                s0 = e_ * c.NT + blk * half
                tv = b.emit('vector', lambda E, s0=s0: E.tensor_tensor_scan(
                    TOTR[0:1, s0:s0 + half], TOTR[0:1, s0:s0 + half], TOTR[0:1, s0:s0 + half],
                    0.0, ALU.add, ALU.bypass), [tto] + vt[-1:], 'dv')
                tv = b.emit('vector', lambda E, s0=s0, e_=e_, blk=blk: E.tensor_scalar(
                    VROW[0:1, s0 + 1:s0 + half], TOTR[0:1, s0:s0 + half - 1],
                    PBR[0:1, e_ * 2 + blk:e_ * 2 + blk + 1], None, ALU.add), [tv, tpb], 'dv')
                tv = b.emit('vector', lambda E, s0=s0, e_=e_, blk=blk: E.tensor_copy(
                    VROW[0:1, s0:s0 + 1], PBR[0:1, e_ * 2 + blk:e_ * 2 + blk + 1]), [tv], 'dv')
                vt.append(tv)
        pi4, ps4, pf4 = ppB.get()
        mkr = b.emit('tensor', lambda E, ps=ps4: E.matmul(ps[:, 0:c.E * c.NT], tril_s[:, :], CM[:, :],
                                                          start=True, stop=False), [tcm, pf4], 'pe')
        mkr = b.emit('tensor', lambda E, ps=ps4: E.matmul(ps[:, 0:c.E * c.NT], ones1[0:1, 0:128],
                                                          VROW[0:1, :], start=False, stop=True),
                     vt + [mkr], 'pe')
        tr = b.emit('vector', lambda E, ps=ps4: E.tensor_tensor(SCR[:, :], C1[:, :], ps[:, 0:c.E * c.NT],
                                                                ALU.mult), [mkr], 'dv')
        tr = b.emit('vector', lambda E: E.reduce_sum(RK1[:, :], v3(SCR[:, :]), AX.X), [tr], 'dv')
        tr2 = b.emit('vector', lambda E, ps=ps4: E.tensor_tensor(SCR[:, :], C2[:, :], ps[:, 0:c.E * c.NT],
                                                                 ALU.mult), [tr], 'dv')
        tr2 = b.emit('vector', lambda E: E.reduce_sum(RK2[:, :], v3(SCR[:, :]), AX.X), [tr2], 'dv')
        ppB.done(pi4, tr2)
        tr = b.emit('vector', lambda E: E.tensor_scalar(RK1[:, :], RK1[:, :], float(c.CAP), None, ALU.is_lt),
                    [tr], 'dv')
        tr2 = b.emit('vector', lambda E: E.tensor_scalar(RK2[:, :], RK2[:, :], float(c.CAP), None, ALU.is_lt),
                     [tr2], 'dv')
        tg1 = b.emit('vector', lambda E: E.tensor_tensor(SM1[:, :], T1[:, :], T2[:, :], ALU.add), [tk2], 'dv')
        tg1 = b.emit('vector', lambda E: E.reciprocal(SM1[:, :], SM1[:, :]), [tg1], 'dv')
        tg1a = b.emit('vector', lambda E: E.tensor_tensor(T1[:, :], T1[:, :], SM1[:, :], ALU.mult), [tg1], 'dv')
        tg2a = b.emit('vector', lambda E: E.tensor_tensor(T2[:, :], T2[:, :], SM1[:, :], ALU.mult), [tg1a], 'dv')
        tg1a = b.emit('vector', lambda E: E.tensor_tensor(T1[:, :], T1[:, :], RK1[:, :], ALU.mult), [tg1a, tr], 'dv')
        tg2a = b.emit('vector', lambda E: E.tensor_tensor(T2[:, :], T2[:, :], RK2[:, :], ALU.mult), [tg2a, tr2], 'dv')
        tG = b.emit('vector', lambda E: E.tensor_tensor(
            v3(GMAT[:, :]), v3(C1[:, :]),
            T1[:, :].rearrange("p (t o) -> p t o", o=1).broadcast_to([128, c.NT, c.E]), ALU.mult), [tg1a], 'dv')
        tG = b.emit('vector', lambda E: E.tensor_tensor(
            v3(SCR[:, :]), v3(C2[:, :]),
            T2[:, :].rearrange("p (t o) -> p t o", o=1).broadcast_to([128, c.NT, c.E]), ALU.mult), [tg2a, tG], 'dv')
        tG = b.emit('vector', lambda E: E.tensor_tensor(GMAT[:, :], GMAT[:, :], SCR[:, :], ALU.add), [tG], 'dv')
        es.close()
        b.bar()

        es_c.close()
        es = ExitStack()
        sbx = lambda n, shp: es.enter_context(nc.sbuf_tensor(f"f{call}d_{n}", shp, F32))
        w1 = sbx("w1", [128, 8 * c.F])
        w2 = sbx("w2", [128, 16 * D])
        lx_p = Pool([sbx(f"lx{i}", [128, 8 * 512]) for i in range(1)])
        fc1 = sbx("fc1", [128, (c.F // 128) * 512])
        mo_p = Pool([sbx(f"mo{i}", [128, D]) for i in range(2)])
        w1_readers, w2_readers, fc1_readers = [], [], []
        moeb_dmas = {}
        for e_ in range(c.E):
            w1tk = [b.dma('sync', w1[:, d * c.F:(d + 1) * c.F], MOEW1[e_, d * 128:(d + 1) * 128, :],
                          w1_readers, sem='w1') for d in range(8)]
            w2tk = [b.dma('sync', w2[:, f_ * D:(f_ + 1) * D], MOEW2[e_, f_ * 128:(f_ + 1) * 128, :],
                          w2_readers, sem='w2') for f_ in range(16)]
            w1tok, w2tok = w1tk[-1], w2tk[-1]
            w1_readers, w2_readers = [], []
            for t5 in range(c.N5):
                lxi, lx, lxf = lx_p.get()
                tlx = b.dma('sync', lx[:, :].rearrange("p (d s) -> p d s", d=8),
                            LN2T[0:D, t5 * 512:(t5 + 1) * 512].rearrange("(d p) s -> p d s", p=128),
                            [lxf], sem='lx')
                f1tk = []
                prev_fc1_readers = fc1_readers
                fc1_readers = []
                for f_ in range(c.F // 128):
                    pi, ps, pf = ppA.get()
                    mk = None
                    for d in range(8):
                        w8 = [w1tok] if t5 == 0 and f_ == 0 and d == 0 else []
                        mk = b.emit('tensor', lambda E, ps=ps, lx=lx, d=d, f_=f_: E.matmul(
                            ps[:, :], w1[:, d * c.F + f_ * 128:d * c.F + f_ * 128 + 128],
                            lx[:, d * 512:(d + 1) * 512],
                            start=(d == 0), stop=(d == 7)),
                            ([tlx, pf] if d == 0 else []) + w8, 'pe')
                    ev = b.emit('scalar', lambda E, ps=ps, f_=f_: E.activation(
                        fc1[:, f_ * 512:(f_ + 1) * 512], ps[:, :], AF.Gelu_apprx_tanh),
                        [mk] + prev_fc1_readers, 'ac')
                    f1tk.append(ev)
                    ppA.done(pi, ev)
                lx_p.done(lxi, f1tk[-1])
                w1_readers = [f1tk[-1]]
                mo_last = None
                for ts in range(4):
                    moi, mo, mof = mo_p.get()
                    evs = []
                    for o in range(2):
                        pi, ps, pf = ppB.get()
                        mk = None
                        for f_ in range(16):
                            w8 = [w2tok] if t5 == 0 and ts == 0 and o == 0 and f_ == 0 else []
                            mk = b.emit('tensor', lambda E, ps=ps, f_=f_, ts=ts, o=o: E.matmul(
                                ps[:, :], fc1[:, f_ * 512 + ts * 128:f_ * 512 + ts * 128 + 128],
                                w2[:, f_ * D + o * 512:f_ * D + o * 512 + 512],
                                start=(f_ == 0), stop=(f_ == 15)),
                                ([f1tk[f_], pf] if f_ == 0 else [f1tk[f_]]) + w8, 'pe')
                        tile = t5 * 4 + ts
                        ev = b.emit('vector', lambda E, ps=ps, mo=mo, o=o, tile=tile, e_=e_: E.tensor_scalar(
                            mo[:, o * 512:(o + 1) * 512], ps[:, :],
                            GMAT[:, e_ * c.NT + tile:e_ * c.NT + tile + 1], None, ALU.mult),
                            [mk] + (mof if o == 0 else []), 'dv')
                        evs.append(ev)
                        ppB.done(pi, ev)
                    tile = t5 * 4 + ts
                    kw = {} if e_ == 0 else {'accum_op': ALU.add}
                    prev_dma = [moeb_dmas[tile]] if tile in moeb_dmas else []
                    mo_last = b.dma('gpsimd', MOEB[tile * 128:(tile + 1) * 128, :], mo[:, :],
                                    evs[-2:] + prev_dma, sem=f'ma{tile % 4}', **kw)
                    moeb_dmas[tile] = mo_last
                    mo_p.done(moi, mo_last)
                fc1_readers = [mk]
                w2_readers = [mk]
        es.close()
        b.bar()

        es = ExitStack()
        sbx = lambda n, shp: es.enter_context(nc.sbuf_tensor(f"f{call}e_{n}", shp, F32))
        xa_p = Pool([sbx(f"xa{i}", [128, D]) for i in range(2)])
        h1_p = Pool([sbx(f"h1_{i}", [128, D]) for i in range(2)])
        mo_p = Pool([sbx(f"mo{i}", [128, D]) for i in range(2)])
        ot_p = Pool([sbx(f"ot{i}", [128, D]) for i in range(2)])
        for t in range(c.NT):
            xi, xa, xf = xa_p.get()
            hi, h1, hf = h1_p.get()
            mi, mo, mf = mo_p.get()
            t1_ = b.dma('sync', xa[:, :], src[t * 128:(t + 1) * 128, :], [xf], sem=f'f6x{xi}')
            t2_ = b.dma('sync', h1[:, :], H1B[t * 128:(t + 1) * 128, :], [hf], sem=f'f6h{hi}')
            t3_ = b.dma('sync', mo[:, :], MOEB[t * 128:(t + 1) * 128, :], [mf], sem=f'f6m{mi}')
            oi, ot, of = ot_p.get()
            tv = b.emit('vector', lambda E, h1=h1, mo=mo, ot=ot: E.tensor_tensor(
                ot[:, :], h1[:, :], mo[:, :], ALU.add), [t2_, t3_, of], 'dv')
            h1_p.done(hi, tv)
            mo_p.done(mi, tv)
            tv = b.emit('vector', lambda E, ot=ot, xa=xa: E.scalar_tensor_tensor(
                ot[:, :], ot[:, :], 0.5, xa[:, :], ALU.mult, ALU.add), [tv, t1_], 'dv')
            xa_p.done(xi, tv)
            td = b.dma('gpsimd', dst[t * 128:(t + 1) * 128, :], ot[:, :], [tv], sem=f'ots{oi}')
            ot_p.done(oi, td)
        es.close()
        b.bar()

    onescol = sbp("onescol", [128, 1], F32)
    onescw = sbp("onescw", [max(c.W, 2), 1], F32)

    def onesc_l():
        return onescol[:, :]

    def onesc_lw():
        return onescw[:, :]

    b.emit('vector', lambda E: E.memset(onescol[:, :], 1.0), (), 'dv')
    b.emit('vector', lambda E: E.memset(onescw[:, :], 1.0), (), 'dv')

    AGC_IN = [nc.dram_tensor(f"agc_in{k}", [1, 16], F32).ap() for k in range(2)]
    AGC_OUT = [nc.dram_tensor(f"agc_out{k}", [c.W, 16], F32).ap() for k in range(2)]

    PREIDX = nc.dram_tensor("preidx", [c.W], F32, kind="ExternalInput").ap()
    LNMT = nc.dram_tensor("lnmt", [D, R], F32).ap()
    XPRE = nc.dram_tensor("xpre", [c.CONV, c.B * (c.PC + 3)], F32).ap()
    BCFM = nc.dram_tensor("bcfm", [2 * c.DS, R], F32).ap()
    XBTM = nc.dram_tensor("xbtm", [R, c.DI + c.DS], F32).ap()
    ZTM = nc.dram_tensor("ztm", [R, c.DI], F32).ap()
    STB = nc.dram_tensor("statesb", [c.B * c.NC * c.MH * 128, 128], F32).ap()
    PRB = nc.dram_tensor("prevb", [c.B * c.NC, 128, c.MH * 128], F32).ap()
    AGH_IN = nc.dram_tensor("agh_in", [6, D], F32).ap()
    AGH_OUT = nc.dram_tensor("agh_out", [c.W * 6, D], F32).ap()
    AGT_IN = nc.dram_tensor("agt_in", [c.B * 128, c.MH * 128], F32).ap()
    AGT_OUT = nc.dram_tensor("agt_out", [c.W * c.B * 128, c.MH * 128], F32).ap()
    AGD_IN = nc.dram_tensor("agd_in", [1, 32], F32).ap()
    AGD_OUT = nc.dram_tensor("agd_out", [c.W, 32], F32).ap()
    Hd.update(dict(LNMT=LNMT, XPRE=XPRE, BCFM=BCFM, XBTM=XBTM, ZTM=ZTM, STB=STB, PRB=PRB))

    def mamba(src, dst):
        NCC = c.NC
        PT3 = c.PC // 512
        es_m = ExitStack()
        sbm = lambda n, shp: es_m.enter_context(nc.sbuf_tensor(f"m_{n}", shp, F32))
        mnw_b = sbm("mnw_b", [128, c.DI])
        dvec_b = sbm("dvec_b", [128, c.MH])
        dtbias_c = sbm("dtbias_c", [32, 1])
        aneg_c = sbm("aneg_c", [32, 1])
        convw_c = sbm("convw_c", [128, 18 * c.DC])
        convb_c = sbm("convb_c", [128, 18])
        sub0 = sbm("sub0", [128, c.CH])
        sub1 = sbm("sub1", [128, c.CH])
        sel127 = sbm("sel127", [128, 128])
        selh = sbm("selh", [32, c.MH * 128])
        selw = sbm("selw", [max(c.W, 2), max(c.W, 2) * 128])
        cm1m = sbm("cm1m", [128, c.W])
        preidx_c = sbm("preidx_c", [max(c.W, 2), 1])
        ACS = sbm("acs", [32, R])
        ACST = sbm("acst", [128, (R // 128) * 32])
        DTT = sbm("dtt", [128, (R // 128) * 32])
        DCY = sbm("dcy", [128, c.B * NCC * 32])
        es_dt = ExitStack()
        DTR = es_dt.enter_context(nc.sbuf_tensor("m_dtr", [32, R], F32))
        DTV = es_dt.enter_context(nc.sbuf_tensor("m_dtv", [32, R], F32))
        PIDX = es_dt.enter_context(nc.sbuf_tensor("m_pidx", [32, c.MH * 128], F32))
        Hd.update(dict(DTR=DTR, DTV=DTV, ACS=ACS, ACST=ACST, DTT=DTT, DCY=DCY))

        tb1 = bcast(mnw_b, MNW, c.DI, sem='mcc')
        tb2 = bcast(dvec_b, DVEC, c.MH, sem='mcc')
        tz1 = b.emit('vector', lambda E: E.memset(dtbias_c[:, :], 0.0), (), 'dv')
        tz2 = b.emit('vector', lambda E: E.memset(aneg_c[:, :], 0.0), (), 'dv')
        ta = b.dma('sync', dtbias_c[0:c.MH, 0:1], DTBIAS.rearrange("(d o) -> d o", o=1), [tz1], sem='mcb1')
        ta2 = b.dma('sync', aneg_c[0:c.MH, 0:1], ALOG.rearrange("(d o) -> d o", o=1), [tz2], sem='mcb2')
        te = b.emit('scalar', lambda E: E.activation(aneg_c[0:c.MH, 0:1], aneg_c[0:c.MH, 0:1], AF.Exp), [ta2], 'ac')
        aneg_tok = b.emit('scalar', lambda E: E.mul(aneg_c[0:c.MH, 0:1], aneg_c[0:c.MH, 0:1], -1.0), [te], 'ac')
        cwt = [b.dma('sync', convw_c[:, ch * c.DC:(ch + 1) * c.DC], CONVW[ch * 128:(ch + 1) * 128, :], sem='mcw')
               for ch in range(18)]
        cbt = b.dma('sync', convb_c[:, 0:18], CONVB.rearrange("(ch p) -> p ch", p=128), sem='mcw',
                    allow_slow_non_contiguous=True)
        cwt = cbt
        tt = b.emit('vector', lambda E: E.tensor_tensor(sub0[:, 0:128], iot_c[:, :], iot_r[:, :], ALU.is_lt), (), 'dv')
        sub_tok = b.emit('scalar', lambda E: E.mul(sub0[:, 0:128], sub0[:, 0:128], -1e9), [tt], 'ac')
        if c.CH > 128:
            b.emit('vector', lambda E: E.memset(sub0[:, 128:256], 0.0), (), 'dv')
            b.emit('vector', lambda E: E.memset(sub1[:, 0:128], -1e9), (), 'dv')
            tt = b.emit('vector', lambda E: E.tensor_tensor(sub1[:, 128:256], iot_c[:, :], iot_r[:, :], ALU.is_lt), (), 'dv')
            sub_tok = b.emit('scalar', lambda E: E.mul(sub1[:, 128:256], sub1[:, 128:256], -1e9), [sub_tok, tt], 'ac')
        t1 = b.emit('vector', lambda E: E.tensor_scalar(sel127[:, :], iot_r[:, :], 127.0, None, ALU.is_equal), (), 'dv')
        th1 = b.emit('gpsimd', lambda E: E.iota(selh[:, :], [[1, c.MH], [0, 128]], channel_multiplier=0,
                                                allow_small_or_imprecise_dtypes=True), (), 'gp')
        th2 = b.emit('gpsimd', lambda E: E.iota(selw[:, :], [[1, max(c.W, 2)], [0, 128]], channel_multiplier=0,
                                                allow_small_or_imprecise_dtypes=True), (), 'gp')
        thr = b.emit('gpsimd', lambda E: E.iota(PIDX[:, :], [[0, c.MH], [0, 128]], channel_multiplier=1,
                                                allow_small_or_imprecise_dtypes=True), (), 'gp')
        th1 = b.emit('vector', lambda E: E.tensor_tensor(selh[:, :], selh[:, :], PIDX[0:32, :], ALU.is_equal),
                     [th1, thr], 'dv')
        th2 = b.emit('vector', lambda E: E.tensor_tensor(selw[:, :], selw[:, :],
                                                         PIDX[0:max(c.W, 2), 0:max(c.W, 2) * 128], ALU.is_equal),
                     [th2, thr], 'dv')
        tcm = b.emit('vector', lambda E: E.tensor_scalar(cm1m[:, :], cmask_b[:, :], 1.0, None, ALU.subtract), (), 'dv')
        tcm = b.emit('scalar', lambda E: E.mul(cm1m[:, :], cm1m[:, :], -1.0), [tcm], 'ac')
        tpre = b.dma('sync', preidx_c[0:c.W, :], PREIDX.rearrange("(w o) -> w o", o=1), sem='mcb3')
        b.bar()

        es = ExitStack()
        sbx = lambda n, shp: es.enter_context(nc.sbuf_tensor(f"m0_{n}", shp, F32))
        hx = sbx("hx", [128, D])
        hrow = sbx("hrow", [max(c.W, 2), 6 * D])
        hsel = sbx("hsel", [1, 6 * D])
        mn = sbx("mn", [128, 1]); ssq = sbx("ssq", [128, 1]); rstd = sbx("rstd", [128, 1])
        sqt = sbx("sqt", [128, D])
        hxT = sbx("hxT", [128, D])
        tmz = b.emit('vector', lambda E: E.memset(hx[:, :], 1.0), (), 'dv')
        if c.W > 1:
            t0 = b.dma('gpsimd', AGH_IN[0:3, :], src[c.PC - 3:c.PC, :], sem='agi')
            t0b = b.dma('gpsimd', AGH_IN[3:6, :], src[R - 3:R, :], sem='agi')
            tg = b.emit('gpsimd', lambda E: E.collective_compute(
                "AllGather", ALU.bypass, replica_groups=[list(range(c.W))],
                ins=[AGH_IN.opt()], outs=[AGH_OUT.opt()]), [t0, t0b], 'cc')
            tg = b.dma('sync', hrow[0:c.W, :], AGH_OUT.rearrange("(w r) d -> w (r d)", w=c.W), [tg], sem='ago')
            hse = []
            for q in range((6 * D + 511) // 512):
                n0 = q * 512
                n1 = min(6 * D, n0 + 512)
                pi, ps, pf = ppB.get()
                mk = b.emit('tensor', lambda E, ps=ps, n0=n0, n1=n1: E.matmul(
                    ps[0:1, 0:n1 - n0], preidx_c[0:c.W, :], hrow[0:c.W, n0:n1],
                    start=True, stop=True), [tg, tpre, pf], 'pe')
                ev = b.emit('vector', lambda E, ps=ps, n0=n0, n1=n1: E.tensor_copy(
                    hsel[0:1, n0:n1], ps[0:1, 0:n1 - n0]), [mk], 'dv')
                hse.append(ev)
                ppB.done(pi, ev)
            th = b.dma('gpsimd', AGH_IN.rearrange("r d -> (r d)").rearrange("(o x) -> o x", o=1),
                       hsel[0:1, :], hse + [t0, t0b], sem='hxl')
            th = b.dma('gpsimd', hx[0:6, :], AGH_IN[:, :], [th, tmz], sem='hxl')
        else:
            th = b.emit('vector', lambda E: E.memset(hx[0:6, :], 0.0), [tmz], 'dv')
        tn = ln_na(hx[:, :], hx[:, :], mn[:, 0:1], ssq[:, 0:1], sqt[:, :], rstd[:, 0:1], [th])
        httk = [transpose128(hx[:, d * 128:(d + 1) * 128], hxT[:, d * 128:(d + 1) * 128], [tn])
                for d in range(8)]
        wslab = sbx("wslab", [128, 8 * 128])
        hout = sbx("hout", [128, 8])
        for oc in range(18):
            wst = [b.dma('sync', wslab[:, d * 128:(d + 1) * 128],
                         WINT[d * 128:(d + 1) * 128, c.DI + oc * 128:c.DI + (oc + 1) * 128],
                         [] if oc == 0 else [hv], sem='mws') for d in range(8)]
            pi, ps, pf = ppB.get()
            mk = None
            for d in range(8):
                mk = b.emit('tensor', lambda E, ps=ps, d=d: E.matmul(
                    ps[:, 0:128], wslab[:, d * 128:(d + 1) * 128], hxT[:, d * 128:(d + 1) * 128],
                    start=(d == 0), stop=(d == 7)),
                    [wst[-1], pf, httk[d]] if d == 0 else [httk[d]], 'pe')
            hv = b.emit('vector', lambda E, ps=ps: E.tensor_copy(hout[:, 0:6], ps[:, 0:6]),
                        [mk] + ([] if oc == 0 else [hd1, hd2]), 'dv')
            ppB.done(pi, hv)
            hd1 = b.dma('gpsimd', XPRE[oc * 128:(oc + 1) * 128, 0:3], hout[:, 0:3], [hv], sem='hxs')
            hd2 = b.dma('gpsimd', XPRE[oc * 128:(oc + 1) * 128, c.PC + 3:c.PC + 6], hout[:, 3:6], [hv], sem='hxs')
        es.close()
        b.bar()

        es = ExitStack()
        sbx = lambda n, shp: es.enter_context(nc.sbuf_tensor(f"m1_{n}", shp, F32))
        xa_p = Pool([sbx(f"xa{i}", [128, D]) for i in range(2)])
        lm_p = Pool([sbx(f"lm{i}", [128, D]) for i in range(2)])
        lT_p = Pool([sbx(f"lT{i}", [128, D]) for i in range(2)])
        mn = sbx("mn", [128, 1]); ssq = sbx("ssq", [128, 1]); rstd = sbx("rstd", [128, 1])
        sqt = sbx("sqt", [128, D])
        ln_prev = None
        for t in range(c.NT):
            xi, xa, xf = xa_p.get()
            tl = b.dma('sync', xa[:, :], src[t * 128:(t + 1) * 128, :], [xf], sem=f'm1x{xi}')
            li, lm, lf = lm_p.get()
            tn = ln_na(xa[:, :], lm[:, :], mn[:, 0:1], ssq[:, 0:1], sqt[:, :], rstd[:, 0:1], [tl, lf],
                       prev=ln_prev)
            ln_prev = tn
            xa_p.done(xi, tn)
            ti, lT, tf = lT_p.get()
            ttk = [transpose128(lm[:, d * 128:(d + 1) * 128], lT[:, d * 128:(d + 1) * 128], [tn, tf])
                   for d in range(8)]
            lm_p.done(li, ttk[-1])
            ts = b.dma('gpsimd', LNMT[0:D, t * 128:(t + 1) * 128].rearrange("(d p) s -> p d s", p=128),
                       lT[:, :].rearrange("p (d s) -> p d s", d=8), ttk, sem=f'm1s{ti}')
            lT_p.done(ti, ts)
        es.close()
        b.bar()

        es = ExitStack()
        sbx = lambda n, shp: es.enter_context(nc.sbuf_tensor(f"m1a_{n}", shp, F32))
        wz = sbx("wz", [128, 8 * 512])
        lb_p = Pool([sbx(f"lb{i}", [128, 8 * 512]) for i in range(2)])
        zt_p = Pool([sbx(f"zt{i}", [128, 512]) for i in range(2)])
        wz_readers = []
        for o5 in range(c.DI // 512):
            wzt = [b.dma('sync', wz[:, d * 512:(d + 1) * 512],
                         WINT[d * 128:(d + 1) * 128, o5 * 512:(o5 + 1) * 512], wz_readers, sem='mwz')
                   for d in range(8)]
            wztok = wzt[-1]
            for t5 in range(c.N5):
                lbi, lb, lbf = lb_p.get()
                tlb = b.dma('sync', lb[:, :].rearrange("p (d s) -> p d s", d=8),
                            LNMT[0:D, t5 * 512:(t5 + 1) * 512].rearrange("(d p) s -> p d s", p=128),
                            [lbf], sem=f'm1l{lbi}')
                for ts in range(4):
                    pi, ps, pf = ppA.get()
                    mk = None
                    for d in range(8):
                        w8 = [wztok] if t5 == 0 and ts == 0 and d == 0 else []
                        mk = b.emit('tensor', lambda E, ps=ps, lb=lb, d=d, ts=ts: E.matmul(
                            ps[:, :], lb[:, d * 512 + ts * 128:d * 512 + ts * 128 + 128],
                            wz[:, d * 512:(d + 1) * 512],
                            start=(d == 0), stop=(d == 7)),
                            ([tlb, pf] if d == 0 else []) + w8, 'pe')
                    zi, zt, zf = zt_p.get()
                    ev = b.emit('vector', lambda E, zt=zt, ps=ps: E.tensor_copy(zt[:, :], ps[:, :]),
                                [mk] + zf, 'dv')
                    ppA.done(pi, ev)
                    tile = t5 * 4 + ts
                    tzs = b.dma('gpsimd', ZTM[tile * 128:(tile + 1) * 128, o5 * 512:(o5 + 1) * 512],
                                zt[:, :], [ev], sem=f'm1z{zi}')
                    zt_p.done(zi, tzs)
                wz_readers = [mk]
                lb_p.done(lbi, mk)
        es.close()
        b.bar()

        es = ExitStack()
        sbx = lambda n, shp: es.enter_context(nc.sbuf_tensor(f"m1b_{n}", shp, F32))
        wx = sbx("wx", [128, 8 * (c.CONV + c.MH)])
        lb_p = Pool([sbx(f"lb{i}", [128, 8 * 512]) for i in range(2)])
        xo_p = Pool([sbx(f"xo{i}", [128, 512]) for i in range(4)])
        WXC = c.CONV + c.MH
        wxt = [b.dma('sync', wx[:, d * WXC:(d + 1) * WXC], WINT[d * 128:(d + 1) * 128, c.DI:c.DIP], sem='mwx')
               for d in range(8)]
        wxtok = wxt[-1]
        dtr_toks = []
        for t5 in range(c.N5):
            lbi, lb, lbf = lb_p.get()
            tlb = b.dma('sync', lb[:, :].rearrange("p (d s) -> p d s", d=8),
                        LNMT[0:D, t5 * 512:(t5 + 1) * 512].rearrange("(d p) s -> p d s", p=128),
                        [lbf], sem=f'm2l{lbi}')
            bblk = (t5 * 512) // c.PC
            pp0 = t5 * 512 - bblk * c.PC
            col0 = bblk * (c.PC + 3) + 3 + pp0
            lb_last = None
            for oc in range(19):
                ow = 128 if oc < 18 else c.MH
                pi, ps, pf = ppA.get()
                mk = None
                for d in range(8):
                    w8 = [wxtok] if t5 == 0 and oc == 0 and d == 0 else []
                    mk = b.emit('tensor', lambda E, ps=ps, lb=lb, d=d, oc=oc, ow=ow: E.matmul(
                        ps[0:ow, 0:512], wx[:, d * WXC + oc * 128:d * WXC + oc * 128 + ow],
                        lb[:, d * 512:(d + 1) * 512],
                        start=(d == 0), stop=(d == 7)),
                        ([tlb, pf] if d == 0 else []) + w8, 'pe')
                lb_last = mk
                if oc < 18:
                    xi, xo, xf = xo_p.get()
                    ev = b.emit('vector', lambda E, xo=xo, ps=ps: E.tensor_copy(xo[:, :], ps[:, 0:512]),
                                [mk] + xf, 'dv')
                    ppA.done(pi, ev)
                    txs = b.dma('gpsimd', XPRE[oc * 128:(oc + 1) * 128, col0:col0 + 512], xo[:, :],
                                [ev], sem=f'm2x{xi}')
                    xo_p.done(xi, txs)
                else:
                    ev = b.emit('vector', lambda E, ps=ps, t5=t5: E.tensor_copy(
                        DTR[0:c.MH, t5 * 512:(t5 + 1) * 512], ps[0:c.MH, 0:512]),
                        [mk] + dtr_toks, 'dv')
                    ppA.done(pi, ev)
                    dtr_toks = []
            lb_p.done(lbi, lb_last)
            dtr_toks.append(ev)
        b.bar()
        es.close()

        es = ExitStack()
        sbx = lambda n, shp: es.enter_context(nc.sbuf_tensor(f"m3_{n}", shp, F32))
        tz = b.emit('vector', lambda E: E.memset(DTV[:, :], 0.0), (), 'dv')
        tz2 = b.emit('vector', lambda E: E.memset(ACS[:, :], 0.0), (), 'dv')
        t1 = b.emit('scalar', lambda E: E.activation(DTV[0:c.MH, :], DTR[0:c.MH, :], AF.Exp,
                                                     bias=dtbias_c[0:c.MH, 0:1]), dtr_toks + [tz], 'ac')
        t1 = b.emit('scalar', lambda E: E.activation(DTV[0:c.MH, :], DTV[0:c.MH, :], AF.Ln, bias=1.0), [t1], 'ac')
        t2 = b.emit('vector', lambda E: E.tensor_scalar(ACS[0:c.MH, :], DTV[0:c.MH, :],
                                                        aneg_c[0:c.MH, 0:1], None, ALU.mult),
                    [t1, tz2, aneg_tok], 'dv')
        sc_toks = []
        for bb in range(c.B):
            for cc in range(NCC):
                col = bb * c.PC + cc * c.CH
                sc_toks.append(b.emit('vector', lambda E, col=col: E.tensor_tensor_scan(
                    ACS[0:32, col:col + c.CH], ACS[0:32, col:col + c.CH], ACS[0:32, col:col + c.CH],
                    0.0, ALU.add, ALU.bypass), [t2], 'dv'))
        tr_toks = []
        for tt in range(R // 128):
            ta_ = [b.emit('vector', lambda E, tt=tt, q=q: E.transpose(
                ACST[q * 32:(q + 1) * 32, tt * 32:(tt + 1) * 32],
                ACS[0:32, tt * 128 + q * 32:tt * 128 + (q + 1) * 32]), sc_toks, 'dv')
                for q in range(4)]
            td_ = [b.emit('vector', lambda E, tt=tt, q=q: E.transpose(
                DTT[q * 32:(q + 1) * 32, tt * 32:(tt + 1) * 32],
                DTV[0:32, tt * 128 + q * 32:tt * 128 + (q + 1) * 32]), [t1, tz], 'dv')
                for q in range(4)]
            tr_toks.append(ta_[-1])
            tr_toks.append(td_[-1])
        acst_tok = tr_toks[-2]
        dtt_tok = tr_toks[-1]
        dcy_toks = []
        for bb in range(c.B):
            for cc in range(NCC):
                base = (bb * c.PC + cc * c.CH) // 128
                lastt = base + c.CH // 128 - 1
                blk = (bb * NCC + cc) * 32
                pi, ps, pf = ppB.get()
                mk = b.emit('tensor', lambda E, ps=ps, lastt=lastt: E.matmul(
                    ps[:, 0:32], sel127[:, :], ACST[:, lastt * 32:(lastt + 1) * 32],
                    start=True, stop=True), [pf, tr_toks[2 * lastt], t1], 'pe')
                ev = b.emit('scalar', lambda E, ps=ps, blk=blk: E.activation(
                    DCY[:, blk:blk + 32], ps[:, 0:32], AF.Exp), [mk], 'ac')
                dcy_toks.append(ev)
                ppB.done(pi, ev)
        cv_p = Pool([sbx(f"cv{i}", [128, 515]) for i in range(2)])
        co_p = Pool([sbx(f"co{i}", [128, 512]) for i in range(2)])
        ct_p = Pool([sbx(f"ct{i}", [128, 512]) for i in range(2)])
        for fc in range(18):
            for bb in range(c.B):
                for t5 in range(PT3):
                    ci, cv, cf = cv_p.get()
                    col0 = bb * (c.PC + 3) + t5 * 512
                    tl = b.dma('sync', cv[:, :], XPRE[fc * 128:(fc + 1) * 128, col0:col0 + 515],
                               [cf], sem=f'cv{ci}')
                    oi, co, of = co_p.get()
                    tk = b.emit('vector', lambda E, cv=cv, co=co, fc=fc: E.tensor_scalar(
                        co[:, :], cv[:, 0:512], convw_c[:, fc * 4:fc * 4 + 1], None, ALU.mult),
                        [tl, cwt] + of, 'dv')
                    for tap in range(1, 4):
                        tk = b.emit('vector', lambda E, cv=cv, co=co, fc=fc, tap=tap: E.scalar_tensor_tensor(
                            co[:, :], cv[:, tap:tap + 512], convw_c[:, fc * 4 + tap:fc * 4 + tap + 1],
                            co[:, :], ALU.mult, ALU.add), [tk], 'dv')
                    cv_p.done(ci, tk)
                    tk = b.emit('scalar', lambda E, co=co, fc=fc: E.activation(
                        co[:, :], co[:, :], AF.Silu, bias=convb_c[:, fc:fc + 1]), [tk, cbt], 'ac')
                    rr = bb * c.PC + t5 * 512
                    if fc >= 16:
                        co_p.done(oi, b.dma('gpsimd', BCFM[(fc - 16) * 128:(fc - 15) * 128, rr:rr + 512],
                                            co[:, :], [tk], sem=f'cb{oi}'))
                    if fc <= 16:
                        ti2, ct, tf2 = ct_p.get()
                        ttk = [transpose128(co[:, q * 128:(q + 1) * 128], ct[:, q * 128:(q + 1) * 128],
                                            [tk] + tf2) for q in range(4)]
                        if fc < 16:
                            co_p.done(oi, ttk[-1])
                        colx = fc * 128 if fc < 16 else c.DI
                        tds = [b.dma('gpsimd', XBTM[rr + q * 128:rr + (q + 1) * 128, colx:colx + 128],
                                     ct[:, q * 128:(q + 1) * 128], [ttk[q]], sem=f'cx{ti2}') for q in range(4)]
                        ct_p.done(ti2, tds[-1])
        es.close()
        es_dt.close()
        b.bar()

        es = ExitStack()
        sbx = lambda n, shp: es.enter_context(nc.sbuf_tensor(f"m4_{n}", shp, F32))
        btm_p = Pool([sbx(f"btm{i}", [128, 2 * 128]) for i in range(2)])
        xtm_p = Pool([sbx(f"xtm{i}", [128, 2 * c.DI]) for i in range(2)])
        dcyt = sbx("dcyt", [128, 2 * 32])
        bd = sbx("bd", [128, 2 * 128])
        xdt = sbx("xdt", [128, 2 * 128])
        sto_p = Pool([sbx(f"sto{i}", [128, 128]) for i in range(3)])
        dcyt_tok = []
        bd_tok = []
        xdt_tok = []
        for bb in range(c.B):
            for cc in range(NCC):
                r0 = bb * c.PC + cc * c.CH
                base = r0 // 128
                bi, btm, bf = btm_p.get()
                tb = [b.dma('sync', btm[:, ss * 128:(ss + 1) * 128],
                            XBTM[r0 + ss * 128:r0 + (ss + 1) * 128, c.DI:c.DI + 128], [bf], sem=f'bt{bi}')
                      for ss in range(2)]
                xi, xtm, xf = xtm_p.get()
                tx = [b.dma('sync', xtm[:, ss * c.DI:(ss + 1) * c.DI],
                            XBTM[r0 + ss * 128:r0 + (ss + 1) * 128, 0:c.DI], [xf], sem=f'xt{xi}')
                      for ss in range(2)]
                pi, psb, pfb = ppB.get()
                mkb = b.emit('tensor', lambda E, ps=psb, base=base: E.matmul(
                    ps[:, 0:32], sel127[:, :], ACST[:, (base + 1) * 32:(base + 2) * 32],
                    start=True, stop=True), [acst_tok, pfb], 'pe')
                dt_t = []
                for ss in range(2):
                    tkk = b.emit('vector', lambda E, ps=psb, base=base, ss=ss: E.tensor_tensor(
                        dcyt[:, ss * 32:(ss + 1) * 32], ps[:, 0:32],
                        ACST[:, (base + ss) * 32:(base + ss + 1) * 32], ALU.subtract),
                        [mkb] + dcyt_tok, 'dv')
                    tkk = b.emit('scalar', lambda E, ss=ss: E.activation(
                        dcyt[:, ss * 32:(ss + 1) * 32], dcyt[:, ss * 32:(ss + 1) * 32], AF.Exp), [tkk], 'ac')
                    dt_t.append(tkk)
                ppB.done(pi, dt_t[-1])
                dcyt_tok = dt_t
                for h in range(c.MH):
                    tb1 = [b.emit('vector', lambda E, ss=ss, h=h, btm=btm: E.tensor_scalar(
                        bd[:, ss * 128:(ss + 1) * 128], btm[:, ss * 128:(ss + 1) * 128],
                        dcyt[:, ss * 32 + h:ss * 32 + h + 1], None, ALU.mult),
                        [tb[-1], dt_t[ss]] + bd_tok, 'dv') for ss in range(2)]
                    tx1 = [b.emit('vector', lambda E, ss=ss, h=h, xtm=xtm, base=base: E.tensor_scalar(
                        xdt[:, ss * 128:(ss + 1) * 128], xtm[:, ss * c.DI + h * 128:ss * c.DI + h * 128 + 128],
                        DTT[:, (base + ss) * 32 + h:(base + ss) * 32 + h + 1], None, ALU.mult),
                        [tx[-1], dtt_tok] + xdt_tok, 'dv') for ss in range(2)]
                    pi, ps, pf = ppA.get()
                    mk = None
                    for ss in range(2):
                        mk = b.emit('tensor', lambda E, ps=ps, ss=ss: E.matmul(
                            ps[:, 0:128], bd[:, ss * 128:(ss + 1) * 128], xdt[:, ss * 128:(ss + 1) * 128],
                            start=(ss == 0), stop=(ss == 1)),
                            [tb1[ss], tx1[ss]] + ([pf] if ss == 0 else []), 'pe')
                    bd_tok = [mk]
                    xdt_tok = [mk]
                    si, sto, sf = sto_p.get()
                    ev = b.emit('vector', lambda E, sto=sto, ps=ps: E.tensor_copy(sto[:, :], ps[:, 0:128]),
                                [mk] + sf, 'dv')
                    ppA.done(pi, ev)
                    row0 = ((bb * NCC + cc) * c.MH + h) * 128
                    tsd = b.dma('gpsimd', STB[row0:row0 + 128, :], sto[:, :], [ev], sem=f'st{si}')
                    sto_p.done(si, tsd)
                btm_p.done(bi, mk)
                xtm_p.done(xi, mk)
        es.close()
        b.bar()

        es = ExitStack()
        sbx = lambda n, shp: es.enter_context(nc.sbuf_tensor(f"m5_{n}", shp, F32))
        carry = sbx("carry", [128, c.MH * 128])
        stt_p = Pool([sbx(f"stt{i}", [128, c.MH * 128]) for i in range(2)])
        dprod = sbx("dprod", [128, c.B * 32])
        agd_sb = sbx("agd_sb", [max(c.W, 2), 32])
        djj = sbx("djj", [128, 32])
        carry_free = bd_tok + xdt_tok

        def local_scan(bb, init_toks, save_prev):
            toks = list(init_toks)
            for cc in range(NCC):
                r0 = ((bb * NCC + cc) * c.MH) * 128
                sti, stt, stf = stt_p.get()
                tl = b.dma('sync', stt[:, :].rearrange("p (h n) -> p h n", h=c.MH),
                           STB[r0:r0 + c.MH * 128, :].rearrange("(h p) n -> p h n", p=128),
                           [stf], sem=f'sc{sti}')
                if save_prev:
                    tpv = b.dma('gpsimd', PRB[bb * NCC + cc, :, :], carry[:, :], toks, sem='pv')
                    toks = toks + [tpv]
                blk = (bb * NCC + cc) * 32
                ntoks = []
                for h in range(c.MH):
                    tk = b.emit('vector', lambda E, h=h, stt=stt, blk=blk: E.scalar_tensor_tensor(
                        carry[:, h * 128:(h + 1) * 128], carry[:, h * 128:(h + 1) * 128],
                        DCY[:, blk + h:blk + h + 1], stt[:, h * 128:(h + 1) * 128],
                        ALU.mult, ALU.add), [tl] + toks + dcy_toks, 'dv')
                    ntoks.append(tk)
                toks = ntoks
                stt_p.done(sti, toks[-1])
            return toks

        tdp_tok = []
        if c.W > 1:
            agt_toks = []
            for bb in range(c.B):
                tz = b.emit('vector', lambda E: E.memset(carry[:, :], 0.0), carry_free, 'dv')
                toks = local_scan(bb, [tz], save_prev=False)
                tT = b.dma('gpsimd', AGT_IN[bb * 128:(bb + 1) * 128, :], carry[:, :], toks, sem='agq')
                agt_toks.append(tT)
                carry_free = [tT]
                blk0 = bb * NCC * 32
                tdp = b.emit('vector', lambda E, bb=bb, blk0=blk0: E.tensor_copy(
                    dprod[:, bb * 32:bb * 32 + 32], DCY[:, blk0:blk0 + 32]), dcy_toks + tdp_tok, 'dv')
                for cc in range(1, NCC):
                    tdp = b.emit('vector', lambda E, bb=bb, cc=cc, blk0=blk0: E.tensor_tensor(
                        dprod[:, bb * 32:bb * 32 + 32], dprod[:, bb * 32:bb * 32 + 32],
                        DCY[:, blk0 + cc * 32:blk0 + (cc + 1) * 32], ALU.mult), [tdp], 'dv')
                tdp_tok.append(tdp)
            tdd = b.emit('vector', lambda E: E.tensor_copy(
                agd_sb[0:1, 0:32].rearrange("p (x h) -> p x h", h=16),
                dprod[0:1, :].rearrange("p (x h) -> p x h", h=32)[:, :, 0:16]), tdp_tok, 'dv')
            tgd0 = b.dma('gpsimd', AGD_IN[:, :], agd_sb[0:1, 0:32], [tdd], sem='agr')
            tgt = b.emit('gpsimd', lambda E: E.collective_compute(
                "AllGather", ALU.bypass, replica_groups=[list(range(c.W))],
                ins=[AGT_IN.opt()], outs=[AGT_OUT.opt()]), agt_toks + [tgd0], 'cc')
            tgd = b.emit('gpsimd', lambda E: E.collective_compute(
                "AllGather", ALU.bypass, replica_groups=[list(range(c.W))],
                ins=[AGD_IN.opt()], outs=[AGD_OUT.opt()]), [tgt], 'cc')
            tld = b.dma('sync', agd_sb[0:c.W, :], AGD_OUT[:, :], [tgd, tdd], sem='agu')
        djj_tok = []
        for bb in range(c.B):
            if c.W > 1:
                tz = b.emit('vector', lambda E: E.memset(carry[:, :], 0.0), carry_free, 'dv')
                toks = [tz]
                for j in range(c.W):
                    sti, stt, stf = stt_p.get()
                    tl = b.dma('sync', stt[:, :], AGT_OUT[(j * c.B + bb) * 128:(j * c.B + bb + 1) * 128, :],
                               [stf], sem=f'sc{sti}')
                    pi, ps, pf = ppB.get()
                    mk = b.emit('tensor', lambda E, ps=ps, j=j: E.matmul(
                        ps[:, 0:32], selw[0:c.W, j * 128:j * 128 + 128], agd_sb[0:c.W, :],
                        start=True, stop=True), [tld, th2, pf], 'pe')
                    tdj = b.emit('vector', lambda E, ps=ps, j=j: E.tensor_scalar(
                        djj[:, 0:32], ps[:, 0:32], cmask_b[:, j:j + 1], cm1m[:, j:j + 1],
                        ALU.mult, ALU.add), [mk, tcm] + djj_tok, 'dv')
                    ppB.done(pi, tdj)
                    djj_tok = [tdj]
                    tsc = b.emit('vector', lambda E, stt=stt, j=j: E.tensor_scalar(
                        stt[:, :], stt[:, :], cmask_b[:, j:j + 1], None, ALU.mult), [tl], 'dv')
                    ntoks = []
                    for h in range(c.MH):
                        tk = b.emit('vector', lambda E, h=h, stt=stt, bb=bb: E.scalar_tensor_tensor(
                            carry[:, h * 128:(h + 1) * 128], carry[:, h * 128:(h + 1) * 128],
                            djj[:, bb * 16 + h:bb * 16 + h + 1], stt[:, h * 128:(h + 1) * 128],
                            ALU.mult, ALU.add), [tsc, tdj] + toks, 'dv')
                        ntoks.append(tk)
                    toks = ntoks
                    djj_tok = [toks[-1]]
                    stt_p.done(sti, toks[-1])
            else:
                tz = b.emit('vector', lambda E: E.memset(carry[:, :], 0.0), carry_free, 'dv')
                toks = [tz]
            carry_free = local_scan(bb, toks, save_prev=True)
        es.close()
        b.bar()

        es = ExitStack()
        sbx = lambda n, shp: es.enter_context(nc.sbuf_tensor(f"m6_{n}", shp, F32))
        wo = sbx("wo", [128, 16 * D])
        bfm_p = Pool([sbx(f"bfm{i}", [128, c.CH]) for i in range(2)])
        cfm_p = Pool([sbx(f"cfm{i}", [128, c.CH]) for i in range(2)])
        xtm_p = Pool([sbx(f"xtm{i}", [128, 2 * c.DI]) for i in range(1)])
        prv_p = Pool([sbx(f"prv{i}", [128, c.MH * 128]) for i in range(1)])
        zz_p = Pool([sbx(f"zz{i}", [128, c.DI]) for i in range(1)])
        csqb = sbx("csqb", [128, 2 * c.CH])
        mt = sbx("mt", [128, 2 * c.CH])
        xdt = sbx("xdt", [128, 2 * 128])
        ecs = sbx("ecs", [128, 64])
        yt = sbx("yt", [128, 2 * c.DI])
        ytT = sbx("ytT", [128, c.DI])
        mn = sbx("mn", [128, 1]); rstd = sbx("rstd", [128, 1])
        xr_p = Pool([sbx(f"xr{i}", [128, D]) for i in range(2)])
        oo_p = Pool([sbx(f"oo{i}", [128, D]) for i in range(2)])
        wot = [b.dma('sync', wo[:, f_ * D:(f_ + 1) * D], WOUTT[f_ * 128:(f_ + 1) * 128, :], sem='mwo')
               for f_ in range(16)]
        wotok = wot[-1]
        csq_tok, mt_tok, xdt_tok2, ecs_tok, yt_tok, ytT_tok, ln_prev2 = [], [], [], [], [], [], []
        for bb in range(c.B):
            for cc in range(NCC):
                r0 = bb * c.PC + cc * c.CH
                base = r0 // 128
                qn = c.CH // 128
                bi, bfm, bf = bfm_p.get()
                tbf = b.dma('sync', bfm[:, :], BCFM[0:128, r0:r0 + c.CH], [bf], sem=f'bf{bi}')
                ci2, cfm, cf = cfm_p.get()
                tcf = b.dma('sync', cfm[:, :], BCFM[128:256, r0:r0 + c.CH], [cf], sem=f'cf{ci2}')
                xi, xtm, xf = xtm_p.get()
                tx = [b.dma('sync', xtm[:, ss * c.DI:(ss + 1) * c.DI],
                            XBTM[r0 + ss * 128:r0 + (ss + 1) * 128, 0:c.DI], [xf], sem=f'x5{xi}')
                      for ss in range(qn)]
                pvi, prv, pvf = prv_p.get()
                tpv = b.dma('sync', prv[:, :], PRB[bb * NCC + cc, :, :], [pvf], sem=f'pv{pvi}')
                tec = b.emit('scalar', lambda E, base=base: E.activation(
                    ecs[:, 0:32 * qn], ACST[:, base * 32:(base + qn) * 32], AF.Exp),
                    [acst_tok] + ecs_tok, 'ac')
                ecs_tok = []
                yt_toks = []
                for h in range(c.MH):
                    pi, ps, pf = ppB.get()
                    mk = b.emit('tensor', lambda E, ps=ps, h=h, r0=r0: E.matmul(
                        ps[:, 0:c.CH], selh[0:32, h * 128:h * 128 + 128], ACS[0:32, r0:r0 + c.CH],
                        start=True, stop=True), [th1, pf] + sc_toks[-1:], 'pe')
                    tq0 = b.emit('vector', lambda E, ps=ps: E.tensor_tensor(
                        csqb[:, 0:c.CH], ps[:, 0:c.CH], sub0[:, :], ALU.add), [mk, sub_tok] + csq_tok, 'dv')
                    tq1 = tq0
                    if qn > 1:
                        tq1 = b.emit('vector', lambda E, ps=ps: E.tensor_tensor(
                            csqb[:, c.CH:2 * c.CH], ps[:, 0:c.CH], sub1[:, :], ALU.add), [mk, sub_tok], 'dv')
                    ppB.done(pi, [tq0, tq1])
                    csq_tok = [tq0, tq1]
                    mt_new = []
                    for ss in range(qn):
                        pi2, ps2, pf2 = ppB.get()
                        mk2 = b.emit('tensor', lambda E, ps=ps2, ss=ss, bfm=bfm, cfm=cfm: E.matmul(
                            ps[:, 0:c.CH], bfm[:, ss * 128:(ss + 1) * 128], cfm[:, :],
                            start=True, stop=True), [tbf, tcf, pf2], 'pe')
                        tl1 = b.emit('vector', lambda E, ss=ss, base=base, h=h: E.tensor_scalar(
                            mt[:, ss * c.CH:(ss + 1) * c.CH], csqb[:, ss * c.CH:(ss + 1) * c.CH],
                            ACST[:, (base + ss) * 32 + h:(base + ss) * 32 + h + 1], None, ALU.subtract),
                            ([tq0, tq1][ss:ss + 1]) + mt_tok, 'dv')
                        tl1 = b.emit('scalar', lambda E, ss=ss: E.activation(
                            mt[:, ss * c.CH:(ss + 1) * c.CH], mt[:, ss * c.CH:(ss + 1) * c.CH], AF.Exp),
                            [tl1], 'ac')
                        tl1 = b.emit('vector', lambda E, ss=ss, ps=ps2: E.tensor_tensor(
                            mt[:, ss * c.CH:(ss + 1) * c.CH], mt[:, ss * c.CH:(ss + 1) * c.CH],
                            ps[:, 0:c.CH], ALU.mult), [tl1, mk2], 'dv')
                        mt_new.append(tl1)
                        ppB.done(pi2, tl1)
                    mt_tok = mt_new
                    tx1 = [b.emit('vector', lambda E, ss=ss, h=h, xtm=xtm, base=base: E.tensor_scalar(
                        xdt[:, ss * 128:(ss + 1) * 128], xtm[:, ss * c.DI + h * 128:ss * c.DI + h * 128 + 128],
                        DTT[:, (base + ss) * 32 + h:(base + ss) * 32 + h + 1], None, ALU.mult),
                        [tx[-1], dtt_tok] + xdt_tok2, 'dv') for ss in range(qn)]
                    for qs in range(qn):
                        pi3, ps3, pf3 = ppA.get()
                        pi4, ps4, pf4 = ppA.get()
                        mk3 = None
                        for ss in range(qn):
                            mk3 = b.emit('tensor', lambda E, ps=ps3, ss=ss, qs=qs: E.matmul(
                                ps[:, 0:128], mt[:, ss * c.CH + qs * 128:ss * c.CH + qs * 128 + 128],
                                xdt[:, ss * 128:(ss + 1) * 128],
                                start=(ss == 0), stop=(ss == qn - 1)),
                                [mt_new[ss], tx1[ss]] + ([pf3] if ss == 0 else []), 'pe')
                        mk4 = b.emit('tensor', lambda E, ps=ps4, qs=qs, h=h, cfm=cfm, prv=prv: E.matmul(
                            ps[:, 0:128], cfm[:, qs * 128:(qs + 1) * 128], prv[:, h * 128:(h + 1) * 128],
                            start=True, stop=True), [tcf, tpv, pf4], 'pe')
                        ty = b.emit('vector', lambda E, ps3=ps3, qs=qs, h=h: E.tensor_copy(
                            yt[:, qs * c.DI + h * 128:qs * c.DI + h * 128 + 128], ps3[:, 0:128]),
                            [mk3] + yt_tok, 'dv')
                        ty = b.emit('vector', lambda E, ps4=ps4, qs=qs, h=h: E.scalar_tensor_tensor(
                            yt[:, qs * c.DI + h * 128:qs * c.DI + h * 128 + 128], ps4[:, 0:128],
                            ecs[:, qs * 32 + h:qs * 32 + h + 1],
                            yt[:, qs * c.DI + h * 128:qs * c.DI + h * 128 + 128],
                            ALU.mult, ALU.add), [mk4, tec, ty], 'dv')
                        ty = b.emit('vector', lambda E, qs=qs, h=h, xtm=xtm: E.scalar_tensor_tensor(
                            yt[:, qs * c.DI + h * 128:qs * c.DI + h * 128 + 128],
                            xtm[:, qs * c.DI + h * 128:qs * c.DI + h * 128 + 128],
                            dvec_b[:, h:h + 1],
                            yt[:, qs * c.DI + h * 128:qs * c.DI + h * 128 + 128],
                            ALU.mult, ALU.add), [ty, tb2, tx[-1]], 'dv')
                        yt_toks.append(ty)
                        ppA.done(pi3, ty)
                        ppA.done(pi4, ty)
                    xdt_tok2 = [mk3]
                yt_tok = []
                bfm_p.done(bi, mt_new[-1])
                cfm_p.done(ci2, mk4)
                prv_p.done(pvi, mk4)
                for qs in range(qn):
                    zi, zz, zzf = zz_p.get()
                    tz1 = b.dma('sync', zz[:, :], ZTM[r0 + qs * 128:r0 + (qs + 1) * 128, :], [zzf],
                                sem=f'zz{zi}')
                    tz1 = b.emit('scalar', lambda E, zz=zz: E.activation(zz[:, :], zz[:, :], AF.Silu),
                                 [tz1], 'ac')
                    tg1 = b.emit('vector', lambda E, zz=zz, qs=qs: E.tensor_tensor(
                        yt[:, qs * c.DI:(qs + 1) * c.DI], yt[:, qs * c.DI:(qs + 1) * c.DI],
                        zz[:, :], ALU.mult), [tz1] + yt_toks, 'dv')
                    zz_p.done(zi, tg1)
                    yt_tok.append(tg1)
                    tg2 = b.emit('scalar', lambda E, qs=qs: E.activation(
                        ytT[:, :], yt[:, qs * c.DI:(qs + 1) * c.DI], AF.Square, accum_out=mn[:, 0:1]),
                        [tg1] + ln_prev2 + ytT_tok, 'ac')
                    tg2 = b.emit('scalar', lambda E: E.activation(
                        rstd[:, 0:1], mn[:, 0:1], AF.Sqrt, bias=c.EPS, scale=1.0 / c.DI), [tg2], 'ac')
                    tg2 = b.emit('vector', lambda E: E.reciprocal(rstd[:, 0:1], rstd[:, 0:1]), [tg2], 'dv')
                    ln_prev2 = [tg2]
                    tg3 = b.emit('vector', lambda E, qs=qs: E.tensor_scalar(
                        yt[:, qs * c.DI:(qs + 1) * c.DI], yt[:, qs * c.DI:(qs + 1) * c.DI],
                        rstd[:, 0:1], None, ALU.mult), [tg2], 'dv')
                    tg3 = b.emit('vector', lambda E, qs=qs: E.tensor_tensor(
                        yt[:, qs * c.DI:(qs + 1) * c.DI], yt[:, qs * c.DI:(qs + 1) * c.DI],
                        mnw_b[:, :], ALU.mult), [tg3, tb1], 'dv')
                    yt_tok[-1] = tg3
                    ttk = [transpose128(yt[:, qs * c.DI + f_ * 128:qs * c.DI + f_ * 128 + 128],
                                        ytT[:, f_ * 128:(f_ + 1) * 128], [tg3] + ytT_tok)
                           for f_ in range(16)]
                    ytT_tok = []
                    ri, xr, rf = xr_p.get()
                    trr = b.dma('sync', xr[:, :], src[r0 + qs * 128:r0 + (qs + 1) * 128, :], [rf],
                                sem=f'rr{ri}')
                    oi, oo, of = oo_p.get()
                    for o in range(2):
                        pi5, ps5, pf5 = ppB.get()
                        mk5 = None
                        for f_ in range(16):
                            w8 = [wotok] if bb == 0 and cc == 0 and qs == 0 and o == 0 and f_ == 0 else []
                            mk5 = b.emit('tensor', lambda E, ps=ps5, f_=f_, o=o: E.matmul(
                                ps[:, :], ytT[:, f_ * 128:(f_ + 1) * 128],
                                wo[:, f_ * D + o * 512:f_ * D + o * 512 + 512],
                                start=(f_ == 0), stop=(f_ == 15)),
                                ([ttk[f_], pf5] if f_ == 0 else [ttk[f_]]) + w8, 'pe')
                        ev = b.emit('vector', lambda E, oo=oo, ps=ps5, o=o, xr=xr: E.tensor_tensor(
                            oo[:, o * 512:(o + 1) * 512], ps[:, :], xr[:, o * 512:(o + 1) * 512],
                            ALU.add), [mk5, trr] + (of if o == 0 else []), 'dv')
                        ppB.done(pi5, ev)
                        ytT_tok = [mk5]
                    xr_p.done(ri, ev)
                    tdo = b.dma('gpsimd', dst[r0 + qs * 128:r0 + (qs + 1) * 128, :], oo[:, :], [ev],
                                sem=f'oo{oi}')
                    oo_p.done(oi, tdo)
        es.close()
        es_m.close()
        b.bar()

    consts()
    ffn_call(U, RES1, 0)
    if stub_mamba:
        for q in range(4):
            b.dma('sync', RES2[q * (R // 4):(q + 1) * (R // 4), :], RES1[q * (R // 4):(q + 1) * (R // 4), :])
        b.bar()
    else:
        mamba(RES1, RES2)
    ffn_call(RES2, OUT, 1)

    b.bar()

    b.build()
    return nc, Hd


def _prep_inputs(cfg, core, inp):
    c = cfg
    u = inp["L0_attn_u"]
    sl = u[:, core * c.PC:(core + 1) * c.PC, :].reshape(c.R, c.D)
    d = {
        "u": np.ascontiguousarray(sl, dtype=np.float32),
        "wqkvT": np.ascontiguousarray(inp["L0_ffn_qkv_weight"].T),
        "wprojT": np.ascontiguousarray(inp["L0_ffn_proj_weight"].T),
        "wrouter": np.ascontiguousarray(inp["L0_ffn_router_weight"]),
        "moew1": np.ascontiguousarray(inp["L0_ffn_moe_w1"]),
        "moew2": np.ascontiguousarray(inp["L0_ffn_moe_w2"]),
        "ln1w": np.ascontiguousarray(inp["L0_ffn_ln1_weight"]),
        "ln1b": np.ascontiguousarray(inp["L0_ffn_ln1_bias"]),
        "ln2w": np.ascontiguousarray(inp["L0_ffn_ln2_weight"]),
        "ln2b": np.ascontiguousarray(inp["L0_ffn_ln2_bias"]),
        "winT": np.ascontiguousarray(inp["L0_attn_W_in"].T),
        "woutT": np.ascontiguousarray(inp["L0_attn_W_out"].T),
        "convw": np.ascontiguousarray(inp["L0_attn_conv_w"].reshape(-1, 4)),
        "convb": np.ascontiguousarray(inp["L0_attn_conv_b"]),
        "dtbias": np.ascontiguousarray(inp["L0_attn_dt_bias"]),
        "alog": np.ascontiguousarray(inp["L0_attn_A_log"]),
        "dvec": np.ascontiguousarray(inp["L0_attn_D"]),
        "mnw": np.ascontiguousarray(inp["L0_attn_norm_weight"]),
        "cmask": (np.arange(cfg.W) < core).astype(np.float32),
        "preidx": (np.arange(cfg.W) == core - 1).astype(np.float32),
        "hmask": np.array([0.0 if core == 0 else 1.0], dtype=np.float32),
    }
    return d


def kernel(**inputs):
    cfg = Cfg(world=8, pc=2048)
    nc, _ = build_program(cfg)
    in_maps = [_prep_inputs(cfg, i, inputs) for i in range(cfg.W)]
    res = run_bass_kernel_spmd(nc, in_maps, core_ids=list(range(cfg.W)))
    out = np.empty((2, 16384, 1024), dtype=np.float32)
    for i in range(cfg.W):
        out[:, i * cfg.PC:(i + 1) * cfg.PC, :] = res.results[i]["out"].reshape(2, cfg.PC, 1024)
    return out
